# version 02
# baseline (speedup 1.0000x reference)
import numpy as np
import ml_dtypes
import concourse.bass as bass
import concourse.mybir as mybir
from concourse.bass_utils import run_bass_kernel_spmd

F32 = mybir.dt.float32
BF16 = mybir.dt.bfloat16
AF = mybir.ActivationFunctionType
ALU = mybir.AluOpType
AX = mybir.AxisListType

D = 1024
HID = 4096
E = 8
NSEQ = 2
NCORE = 8
LN_EPS = 1e-5
KB = 1024
NEG = -1.0e30
SKIP = set()


def _rope_perm64():
    p = []
    p += [2 * i for i in range(16)]
    p += [2 * i + 1 for i in range(16)]
    p += [32 + 2 * i for i in range(16)]
    p += [33 + 2 * i for i in range(16)]
    return np.array(p)


def _rope_tables(T):
    freqs = 10000.0 ** (-(np.arange(0, 64, 2, dtype=np.float64) / 64.0))
    t = np.arange(T, dtype=np.float64)
    ang = np.outer(freqs, t)
    cos = np.cos(ang).astype(np.float32)
    sin = np.sin(ang).astype(np.float32)
    A64 = np.zeros((64, T), np.float32)
    B64 = np.zeros((64, T), np.float32)
    A64[0:16] = cos[0:16]
    A64[16:32] = cos[0:16]
    A64[32:48] = cos[16:32]
    A64[48:64] = cos[16:32]
    B64[0:16] = -sin[0:16]
    B64[16:32] = sin[0:16]
    B64[32:48] = -sin[16:32]
    B64[48:64] = sin[16:32]
    A = np.concatenate([A64, A64], 0)
    B = np.concatenate([B64, B64], 0)
    return (np.asarray(A, dtype=ml_dtypes.bfloat16),
            np.asarray(B, dtype=ml_dtypes.bfloat16))


class MK:
    def __init__(self, T):
        self.T = T
        self.nj = T // 512
        self.ntc = T // 128
        nc = bass.Bass("TRN2", target_bir_lowering=False,
               detect_race_conditions=False)
        self.nc = nc
        self.cnt = {}
        self.scs_free = [0, 0]
        self.x1b_free = [(None, 0), (None, 0)]
        self._declare_dram()
        self._alloc_sbuf()
        self._alloc_psum()
        self._sems()
        with nc.Block() as _blk:
            self._build()
        nc.finalize()

    def inc(self, name, n=1):
        self.cnt[name] = self.cnt.get(name, 0) + n
        return self.cnt[name]

    def val(self, name):
        return self.cnt.get(name, 0)

    def w(self, eng, sem, v):
        if v and v > 0:
            eng.wait_ge(self.sem[sem], v)

    def tag(self, ins, sem):
        v = self.inc(sem)
        ins.then_inc(self.sem[sem], 1)
        return v

    def edge(self, eng, ins, sem):
        v = self.tag(ins, sem)
        eng.wait_ge(self.sem[sem], v)
        return v

    def dma(self, eng, sem, out, in_):
        v = self.inc(sem, 16)
        eng.dma_start(out=out, in_=in_).then_inc(self.sem[sem], 16)
        return v

    def barrier(self):
        nc = self.nc
        tgt = self.val("ph") + 5
        for eng in (nc.tensor, nc.vector, nc.scalar, nc.gpsimd, nc.sync):
            self.inc("ph")
            eng.nop().then_inc(self.sem["ph"], 1)
        for eng in (nc.tensor, nc.vector, nc.scalar, nc.gpsimd, nc.sync):
            eng.wait_ge(self.sem["ph"], tgt)

    def _declare_dram(self):
        nc, T = self.nc, self.T
        dt = nc.dram_tensor
        self.d_x = dt("x", [NSEQ, T, D], F32, kind="ExternalInput")
        self.d_wqkvT = dt("wqkvT", [D, 3 * D], F32, kind="ExternalInput")
        self.d_woutT = dt("woutT", [D, D], F32, kind="ExternalInput")
        self.d_w1 = dt("w1", [E, D, HID], F32, kind="ExternalInput")
        self.d_w2 = dt("w2", [E, HID, D], F32, kind="ExternalInput")
        self.d_wg = dt("wg", [D, E], F32, kind="ExternalInput")
        self.d_bg = dt("bg", [1, E], F32, kind="ExternalInput")
        self.d_b1r = dt("b1r", [128, E * 32], F32, kind="ExternalInput")
        self.d_b2 = dt("b2", [E, D], F32, kind="ExternalInput")
        self.d_gam = dt("gam", [1, D], F32, kind="ExternalInput")
        self.d_bet = dt("bet", [1, D], F32, kind="ExternalInput")
        self.d_ropeA = dt("ropeA", [128, T], BF16, kind="ExternalInput")
        self.d_ropeB = dt("ropeB", [128, T], BF16, kind="ExternalInput")
        self.d_tri = dt("tri", [128, 128], BF16, kind="ExternalInput")
        self.d_out = dt("out", [NSEQ, T, D], F32, kind="ExternalOutput")
        self.d_x1sp = dt("x1sp", [NSEQ, T, D], BF16, kind="Internal")

    def _alloc_sbuf(self):
        nc = self.nc
        T, ntc = self.T, self.ntc
        self.ident = nc.alloc_sbuf_tensor("ident", [128, 128], BF16)
        self.blhs = nc.alloc_sbuf_tensor("blhs", [128, 128], BF16)
        self.blhs64 = nc.alloc_sbuf_tensor("blhs64", [128, 128], BF16)
        self.onescol = nc.alloc_sbuf_tensor("onescol", [128, 1], BF16)
        self.onesrf = nc.alloc_sbuf_tensor("onesrf", [1, 128], F32)
        self.scs = nc.alloc_sbuf_tensor("scs", [128, 64], F32)
        self.mk2 = nc.alloc_sbuf_tensor("mk2", [1, 8], F32)
        self.eq8 = nc.alloc_sbuf_tensor("eq8", [128, 32], F32)
        self.comb = nc.alloc_sbuf_tensor("comb", [128, ntc * E], F32)
        self.combbf = nc.alloc_sbuf_tensor("combbf", [128, 8], BF16)
        self.cst = nc.alloc_sbuf_tensor("cst", [128, 4], F32)

        base = nc.sbuf_base
        top = nc.sbuf_top
        cur = [base]

        def at(name, shape, dtype):
            per = 1
            for s in shape[1:]:
                per *= s
            per *= mybir.dt.size(dtype)
            off = (cur[0] + 31) // 32 * 32
            h = nc.alloc_sbuf_tensor_at(name, shape, dtype, offset=off)
            cur[0] = off + per
            assert cur[0] <= top, (name, cur[0], top)
            return h

        cur[0] = base
        self.ropeA = at("ropeA_s", [128, T], BF16)
        self.ropeB = at("ropeB_s", [128, T], BF16)
        self.tri = at("tri_s", [128, 128], BF16)
        self.wq = at("wq", [128, 8 * 3 * D], BF16)
        self.wo = at("wo", [128, 8 * D], BF16)
        self.Kb = at("Kb", [128, 8 * T], BF16)
        self.Vb = at("Vb", [128, ntc * D], BF16)
        self.Pp = at("Pp", [128, ntc * 512], BF16)
        self.ofm = at("ofm", [128, 8 * 512], BF16)
        self.Qp = at("Qp", [128, 8 * 512], BF16)
        self.xn1f = at("xn1f", [128, 8 * 512], BF16)
        self.xbuf = at("xbuf", [128, 2 * 4 * D], BF16)
        shf_off = (cur[0] + 31) // 32 * 32
        self.shf = at("shf", [128, 512], F32)
        self.t1f = at("t1f", [128, 512], F32)
        self.atm = nc.alloc_sbuf_tensor_at("atm", [128, D], F32,
                                           offset=shf_off)
        self.xn1tm = at("xn1tm", [128, D], BF16)
        self.qsq = at("qsq", [128, 1536], BF16)
        self.negb = at("negb", [128, 512], BF16)
        self._a_end = cur[0]

        cur[0] = base
        self.x1b = at("x1b", [128, 2 * D], BF16)
        self.xn2f = at("xn2f", [128, 8 * T], BF16)
        self.accf = at("accf", [128, ntc * D], F32)
        self.w1b = at("w1b", [128, 2 * 8 * KB], BF16)
        self.w2b = at("w2b", [128, 8 * KB], BF16)
        self.gbuf = at("gbuf", [128, 8 * 512], BF16)
        self.gamB = at("gamB", [128, D], F32)
        self.betB = at("betB", [128, D], F32)
        self.b1s = at("b1s", [128, E * 32], F32)
        self.b2s = at("b2s", [8, D], BF16)
        self.wgs = at("wgs", [128, 8 * E], BF16)
        self.bgp = at("bgp", [128, E], BF16)
        self.combT = at("combT", [8, T], BF16)
        self.x2tm = at("x2tm", [128, D], BF16)
        self.rn = at("rn", [128, D], F32)
        self.s1f = at("s1f", [128, D], F32)
        self.s2f = at("s2f", [128, D], F32)
        self.ostg = at("ostg", [128, 2 * D], F32)
        self.gamrow = at("gamrow", [1, D], F32)
        self.betrow = at("betrow", [1, D], F32)
        self._f_end = cur[0]
        self._top = top

    def _alloc_psum(self):
        nc = self.nc
        self.pool = [nc.alloc_psum_tensor("pool%d" % i, [128, 512], F32)
                     for i in range(4)]
        self.pool_state = [None] * 4
        self.pool_i = 0
        self.psPV = nc.alloc_psum_tensor("psPV", [128, 512], F32)
        self.psL0 = nc.alloc_psum_tensor("psL0", [128, 512], F32)
        self.psL1 = nc.alloc_psum_tensor("psL1", [128, 512], F32)
        self.psX = nc.alloc_psum_tensor("psX", [128, 1024], BF16)

    def _sems(self):
        names = ("dC dWq dX0 dX1 dW0 dW1 dW2 dSp dL0 dL1 dOut0 dOut1 "
                 "sT sS sV sG ph").split()
        self.sem = {n: self.nc.alloc_semaphore(name=n) for n in names}

    def pacq(self):
        i = self.pool_i % 4
        self.pool_i += 1
        st = self.pool_state[i]
        if st is not None:
            self.w(self.nc.tensor, st[0], st[1])
            self.pool_state[i] = None
        return i, self.pool[i]

    def prel(self, i, sem_name, v):
        self.pool_state[i] = (sem_name, v)

    def ln(self, src_f, setn):
        nc = self.nc
        s0 = setn * 32
        self.w(nc.vector, "sS", self.scs_free[setn])
        nc.vector.bn_stats(self.scs[:, s0:s0 + 6], src_f(0))
        bs = self.tag(nc.vector.bn_stats(self.scs[:, s0 + 6:s0 + 12],
                                         src_f(1)), "sV")
        self.w(nc.vector, "sV", bs)
        sv = self.tag(nc.vector.bn_aggr(self.scs[:, s0 + 12:s0 + 14],
                                        self.scs[:, s0:s0 + 12]), "sV")
        self.w(nc.scalar, "sV", sv)
        self.edge(nc.scalar,
                  nc.scalar.activation(self.scs[:, s0 + 16:s0 + 17],
                                       self.scs[:, s0 + 13:s0 + 14], AF.Ln,
                                       bias=self.cst[:, 0:1]), "sS")
        ss = self.tag(nc.scalar.activation(self.scs[:, s0 + 14:s0 + 15],
                                           self.scs[:, s0 + 16:s0 + 17],
                                           AF.Exp,
                                           scale=self.cst[:, 1:2]), "sS")
        self.w(nc.vector, "sS", ss)
        sv = self.tag(nc.vector.tensor_scalar(self.scs[:, s0 + 15:s0 + 16],
                                              self.scs[:, s0 + 12:s0 + 13],
                                              self.scs[:, s0 + 14:s0 + 15],
                                              -1.0, ALU.mult, ALU.mult), "sV")
        return sv

    def ln_affine(self, setn, vwait, out_ap, in_ap):
        nc = self.nc
        s0 = setn * 32
        self.w(nc.scalar, "sV", vwait)
        ss = self.tag(nc.scalar.activation(
            out_ap, in_ap, AF.Identity,
            bias=self.scs[:, s0 + 15:s0 + 16],
            scale=self.scs[:, s0 + 14:s0 + 15]), "sS")
        self.scs_free[setn] = ss
        return ss

    def _build(self):
        nc = self.nc
        self._init_consts()
        self.spill_done = {}
        for seq in range(NSEQ):
            self._phase_A(seq)
            self.barrier()
            self._phase_F(seq)
            if seq + 1 < NSEQ:
                self.barrier()
                self.w(nc.sync, "dOut0", self.val("dOut0"))
                self.w(nc.sync, "dOut1", self.val("dOut1"))
        self.w(nc.sync, "dOut0", self.val("dOut0"))
        self.w(nc.sync, "dOut1", self.val("dOut1"))
        self.w(nc.sync, "dSp", self.val("dSp"))

    def _init_consts(self):
        nc = self.nc
        g = nc.gpsimd
        v = nc.vector
        g.iota(self.atm[:, 0:128], [[1, 128]], channel_multiplier=0,
               allow_small_or_imprecise_dtypes=True)
        g.iota(self.atm[:, 128:129], [[1, 1]], channel_multiplier=1,
               allow_small_or_imprecise_dtypes=True)
        g.memset(self.cst[:, 0:1], float(LN_EPS))
        g.memset(self.cst[:, 1:2], -0.5)
        g.memset(self.cst[:, 2:3], 0.125)
        g.memset(self.onescol[:, :], 1.0)
        g.memset(self.onesrf[:, :], 1.0)
        g.memset(self.negb[:, :], 0.0)
        g.memset(self.blhs[:, :], 0.0)
        g.memset(self.blhs64[:, :], 0.0)
        sg = self.tag(g.memset(self.mk2[:, :], 0.0), "sG")
        self.w(v, "sG", sg)
        sv0 = self.tag(v.tensor_scalar(self.ident[:, :], self.atm[:, 0:128],
                                       self.atm[:, 128:129], None,
                                       ALU.is_equal), "sV")
        self.w(v, "sV", sv0)
        sv0 = self.tag(v.memset(self.blhs[0:1, :], 1.0), "sV")
        self.w(v, "sV", sv0)
        sg2 = self.tag(v.memset(self.blhs64[64:65, :], 1.0), "sV")
        for eng in (nc.tensor, nc.scalar, nc.gpsimd, v):
            self.w(eng, "sV", sg2)
            self.w(eng, "sG", sg)

    def _phase_A(self, seq):
        nc = self.nc
        T, nj = self.T, self.nj
        t, v, s, g, y = nc.tensor, nc.vector, nc.scalar, nc.gpsimd, nc.sync

        self.dma(y, "dC", self.ropeA[:, :], self.d_ropeA.ap())
        self.dma(y, "dC", self.ropeB[:, :], self.d_ropeB.ap())
        dc_v = self.dma(y, "dC", self.tri[:, :], self.d_tri.ap())
        wq_in = bass.AP(self.d_wqkvT, 0,
                        [[3 * D, 128], [128 * 3 * D, 8], [1, 3 * D]])
        wq_v = self.dma(g, "dWq",
                        bass.AP(self.wq, 0, [[8 * 3 * D, 128], [3 * D, 8],
                                             [1, 3 * D]]), wq_in)
        wo_in = bass.AP(self.d_woutT, 0, [[D, 128], [128 * D, 8], [1, D]])
        wo_v = self.dma(g, "dWq",
                        bass.AP(self.wo, 0, [[8 * D, 128], [D, 8], [1, D]]),
                        wo_in)

        psx_free = 0
        pp_free = 0
        pv_free = 0
        lrec_free = [0, 0]
        negb_sv = 0
        negb_bc_st = 0
        negb64_ss = 0
        last_odd_st = 0
        last_st_of_c = 0
        for j in range(nj):
            slot = j % 2
            xsl = self.xbuf[:, slot * 4 * D:(slot + 1) * 4 * D]
            dX = "dX0" if slot == 0 else "dX1"
            if j >= 2:
                self.w(g, "dSp", self.spill_done[(seq, j - 2)])
            x_in = bass.AP(self.d_x, (seq * T + j * 512) * D,
                           [[D, 128], [128 * D, 4], [1, D]])
            vx = self.dma(g, dX, xsl, x_in)

            for tcl in range(4):
                setn = tcl % 2
                xc = xsl[:, tcl * D:(tcl + 1) * D]
                if tcl == 0:
                    self.w(v, dX, vx)
                sv = self.ln(lambda hh, xc=xc: xc[:, hh * 512:(hh + 1) * 512],
                             setn)
                ss = self.ln_affine(setn, sv, self.xn1tm[:, :], xc)
                self.w(t, "sS", ss)
                self.w(t, "sS", psx_free)
                for dc in range(8):
                    ins = t.transpose(self.psX[:, dc * 128:(dc + 1) * 128],
                                      self.xn1tm[:, dc * 128:(dc + 1) * 128],
                                      self.ident[:, :])
                st_tr = self.tag(ins, "sT")
                self.w(s, "sT", st_tr)
                out_ap = bass.AP(self.xn1f, tcl * 128,
                                 [[8 * 512, 128], [512, 8], [1, 128]])
                psx_free = self.tag(s.copy(out_ap, self.psX[:, :]), "sS")

            if "qkv" in SKIP:
                continue
            self.w(t, "sS", psx_free)
            if j == 0:
                self.w(t, "dWq", wq_v)
            rope_sv = {}
            for kind in ("q", "k"):
                for c in range(8):
                    oc = c if kind == "q" else 8 + c
                    pi, ps = self.pacq()
                    for dc in range(8):
                        ins = t.matmul(ps[:, :],
                                       self.wq[:, dc * 3072 + oc * 128:
                                               dc * 3072 + oc * 128 + 128],
                                       self.xn1f[:, dc * 512:(dc + 1) * 512],
                                       start=(dc == 0), stop=(dc == 7))
                    st_g = self.tag(ins, "sT")
                    self.w(v, "sT", st_g)
                    v.stream_shuffle(self.shf[:, :], ps[:, :],
                                     list(range(16, 32)) + list(range(16)))
                    v.tensor_tensor(self.t1f[:, :], ps[:, :],
                                    self.ropeA[:, j * 512:(j + 1) * 512],
                                    ALU.mult)
                    v.tensor_tensor(self.shf[:, :], self.shf[:, :],
                                    self.ropeB[:, j * 512:(j + 1) * 512],
                                    ALU.mult)
                    dst = (self.Qp[:, c * 512:(c + 1) * 512] if kind == "q"
                           else self.Kb[:, c * T + j * 512:
                                        c * T + j * 512 + 512])
                    sv = self.tag(v.tensor_add(dst, self.t1f[:, :],
                                               self.shf[:, :]), "sV")
                    rope_sv[(kind, c)] = sv
                    self.prel(pi, "sV", sv)
            for tcl in range(4):
                for vh in range(2):
                    pi, ps = self.pacq()
                    for dc in range(8):
                        ins = t.matmul(
                            ps[:, :],
                            self.xn1f[:, dc * 512 + tcl * 128:
                                      dc * 512 + tcl * 128 + 128],
                            self.wq[:, dc * 3072 + 2048 + vh * 512:
                                    dc * 3072 + 2048 + vh * 512 + 512],
                            start=(dc == 0), stop=(dc == 7))
                    st_g = self.tag(ins, "sT")
                    self.w(s, "sT", st_g)
                    kt = 4 * j + tcl
                    ss = self.tag(s.copy(
                        self.Vb[:, kt * D + vh * 512:kt * D + vh * 512 + 512],
                        ps[:, :]), "sS")
                    self.prel(pi, "sS", ss)

            for c in range(8):
                self.w(s, "sV", rope_sv[("k", c)])
                if j > 0 or c > 0:
                    self.w(s, "sT", last_st_of_c)
                ss = self.tag(s.activation(
                    self.qsq[:, 0:512],
                    self.Kb[:, c * T + j * 512:c * T + j * 512 + 512],
                    AF.Square), "sS")
                self.w(t, "sS", ss)
                pi, ps = self.pacq()
                st_g = self.tag(t.matmul(ps[0:1, :], self.onescol[:, :],
                                         self.qsq[:, 0:512],
                                         start=True, stop=True), "sT")
                last_st_of_c = st_g
                self.w(v, "sT", st_g)
                self.edge(v, v.tensor_reduce(self.scs[0:1, 30:31], ps[0:1, :],
                                             AX.X, ALU.max), "sV")
                sv = self.tag(v.tensor_tensor(self.mk2[0:1, c:c + 1],
                                              self.mk2[0:1, c:c + 1],
                                              self.scs[0:1, 30:31], ALU.max),
                              "sV")
                mk2_sv = sv
                self.prel(pi, "sV", sv)

            nkt = 4 * j + 4
            if "attn" in SKIP:
                nkt = 0
            for c in range(8 if nkt else 0):
                self.w(s, "sV", rope_sv[("q", c)])
                self.w(s, "sT", last_st_of_c)
                ss = self.tag(s.activation(self.qsq[:, 0:512],
                                           self.Qp[:, c * 512:(c + 1) * 512],
                                           AF.Square), "sS")
                self.w(t, "sS", ss)
                pi, ps = self.pacq()
                st_g = self.tag(t.matmul(ps[0:1, :], self.onescol[:, :],
                                         self.qsq[:, 0:512],
                                         start=True, stop=True), "sT")
                last_st_of_c = st_g
                self.w(v, "sT", st_g)
                if negb_bc_st:
                    self.w(v, "sT", negb_bc_st)
                self.w(v, "sV", mk2_sv)
                self.edge(v, v.tensor_scalar(
                    self.scs[0:1, 31:32], self.mk2[0:1, c:c + 1], -768.0,
                    None, ALU.add), "sV")
                negb_sv = self.tag(v.tensor_scalar(
                    self.negb[0:1, :], ps[0:1, :], self.scs[0:1, 31:32],
                    -0.5, ALU.add, ALU.mult), "sV")
                self.prel(pi, "sV", negb_sv)
                pi2, ps2 = self.pacq()
                self.w(t, "sV", negb_sv)
                negb_bc_st = self.tag(t.matmul(ps2[:, :],
                                               self.blhs[0:1, 0:128],
                                               self.negb[0:1, :],
                                               start=True, stop=True), "sT")
                self.w(s, "sT", negb_bc_st)
                if last_odd_st:
                    self.w(s, "sT", last_odd_st)
                negb64_ss = self.tag(s.copy(self.negb[64:65, :],
                                            ps2[64:65, :]), "sS")
                self.prel(pi2, "sS", negb64_ss)

                for par in range(0 if "negbonly" in SKIP else 2):
                    r0, r1 = 64 * par, 64 * par + 64
                    exp_ss = {}
                    tri_sv = {}
                    first = True
                    for kt in range(nkt):
                        m = kt - 4 * j
                        q0 = 128 * m if m >= 0 else 0
                        pi, ps = self.pacq()
                        if first:
                            if par == 0:
                                self.w(t, "sV", negb_sv)
                            else:
                                self.w(t, "sS", negb64_ss)
                            first = False
                        if par == 0:
                            t.matmul(ps[:, q0:512], self.blhs[0:64, 0:128],
                                     self.negb[0:64, q0:512],
                                     start=True, stop=True)
                        else:
                            t.matmul(ps[:, q0:512], self.blhs64[64:128, 0:128],
                                     self.negb[64:128, q0:512],
                                     start=True, stop=True)
                        ins = t.matmul(
                            ps[:, q0:512],
                            self.Kb[r0:r1, c * T + kt * 128:
                                    c * T + kt * 128 + 128],
                            self.Qp[r0:r1, c * 512 + q0:(c + 1) * 512],
                            start=False, stop=True, skip_group_check=True)
                        st_g = self.tag(ins, "sT")
                        if par == 1:
                            last_odd_st = st_g
                        self.w(s, "sT", st_g)
                        if pp_free:
                            self.w(s, "sT", pp_free)
                            pp_free = 0
                        ss = self.tag(s.activation(
                            self.Pp[:, kt * 512 + q0:(kt + 1) * 512],
                            ps[:, q0:512], AF.Exp,
                            scale=self.cst[:, 2:3]), "sS")
                        exp_ss[kt] = ss
                        self.prel(pi, "sS", ss)
                        if m >= 0:
                            self.w(v, "sS", ss)
                            sv = self.tag(v.tensor_tensor(
                                self.Pp[:, kt * 512 + q0:kt * 512 + q0 + 128],
                                self.Pp[:, kt * 512 + q0:kt * 512 + q0 + 128],
                                self.tri[:, :], ALU.mult), "sV")
                            tri_sv[kt] = sv
                    for kt in range(0 if "pv" in SKIP else nkt):
                        m = kt - 4 * j
                        q0 = 128 * m if m >= 0 else 0
                        self.w(t, "sS", exp_ss[kt])
                        if kt in tri_sv:
                            self.w(t, "sV", tri_sv[kt])
                        if kt == 0:
                            if par == 0 and pv_free:
                                self.w(t, "sV", pv_free)
                            if lrec_free[par]:
                                self.w(t, "sV", lrec_free[par])
                        psl = self.psL0 if par == 0 else self.psL1
                        t.matmul(psl[0:1, q0:512], self.onescol[:, :],
                                 self.Pp[:, kt * 512 + q0:(kt + 1) * 512],
                                 start=(kt == 0), stop=(kt == nkt - 1))
                        ins = t.matmul(
                            self.psPV[r0:r1, q0:512],
                            self.Vb[:, kt * D + (2 * c + par) * 64:
                                    kt * D + (2 * c + par) * 64 + 64],
                            self.Pp[:, kt * 512 + q0:(kt + 1) * 512],
                            start=(kt == 0), stop=(kt == nkt - 1))
                        st_g = self.tag(ins, "sT")
                    pp_free = st_g
                    if "pv" in SKIP:
                        continue
                    self.w(v, "sT", st_g)
                    rdst = (self.qsq[0:1, 512:1024] if par == 0
                            else self.qsq[0:1, 1024:1536])
                    psl = self.psL0 if par == 0 else self.psL1
                    with self.nc.allow_low_precision(
                            "1/l feeds a bf16 matmul operand; bf16 rounding"
                            " happens there regardless"):
                        ins = v.reciprocal(rdst, psl[0:1, 0:512])
                    lrec_free[par] = self.tag(ins, "sV")
                if "oev" in SKIP:
                    continue
                pi, ps = self.pacq()
                self.w(t, "sV", lrec_free[0])
                self.w(t, "sV", lrec_free[1])
                t.matmul(ps[0:64, :], self.blhs[0:1, 0:64],
                         self.qsq[0:1, 512:1024], start=True, stop=True)
                st_g = self.tag(t.matmul(ps[64:128, :], self.blhs[0:1, 0:64],
                                         self.qsq[0:1, 1024:1536],
                                         start=True,
                                         stop=True, skip_group_check=True),
                                "sT")
                self.w(s, "sT", st_g)
                if pv_free:
                    self.w(s, "sV", pv_free)
                ss = self.tag(s.copy(self.t1f[:, :], ps[:, :]), "sS")
                self.prel(pi, "sS", ss)
                self.w(v, "sS", ss)
                sv = self.tag(v.tensor_tensor(
                    self.ofm[:, c * 512:(c + 1) * 512],
                    self.psPV[:, 0:512], self.t1f[:, :], ALU.mult), "sV")
                pv_free = sv

            if "wout" in SKIP:
                continue
            if j == 0:
                self.w(t, "dWq", wo_v)
            self.w(t, "sV", pv_free)
            for tcl in range(4):
                evs = []
                for dh in range(2):
                    pi, ps = self.pacq()
                    for dc in range(8):
                        ins = t.matmul(
                            ps[:, :],
                            self.ofm[:, dc * 512 + tcl * 128:
                                     dc * 512 + tcl * 128 + 128],
                            self.wo[:, dc * D + dh * 512:
                                    dc * D + dh * 512 + 512],
                            start=(dc == 0), stop=(dc == 7))
                    st_g = self.tag(ins, "sT")
                    self.w(s, "sT", st_g)
                    ss = self.tag(s.copy(self.atm[:, dh * 512:dh * 512 + 512],
                                         ps[:, :]), "sS")
                    evs.append(ss)
                    self.prel(pi, "sS", ss)
                setn = tcl % 2
                self.w(v, "sS", evs[-1])
                sv = self.ln(lambda hh: self.atm[:, hh * 512:(hh + 1) * 512],
                             setn)
                ss = self.ln_affine(setn, sv, self.xn1tm[:, :], self.atm[:, :])
                self.w(v, "sS", ss)
                xc = xsl[:, tcl * D:(tcl + 1) * D]
                sv = self.tag(v.tensor_tensor(xc, xc, self.xn1tm[:, :],
                                              ALU.add), "sV")
            self.w(y, "sV", sv)
            sp = self.dma(y, "dSp",
                          bass.AP(self.d_x1sp, (seq * T + j * 512) * D,
                                  [[D, 128], [128 * D, 4], [1, D]]),
                          xsl)
            self.spill_done[(seq, j)] = sp
            if "x1out" in SKIP:
                self.dma(g, "dX0" if slot == 0 else "dX1",
                         bass.AP(self.d_out, (seq * T + j * 512) * D,
                                 [[D, 128], [128 * D, 4], [1, D]]),
                         xsl)

    def _phase_F(self, seq):
        nc = self.nc
        T, nj, ntc = self.T, self.nj, self.ntc
        t, v, s, g, y = nc.tensor, nc.vector, nc.scalar, nc.gpsimd, nc.sync
        niter = E * 4

        self.dma(y, "dC", self.gamrow[:, :], self.d_gam.ap())
        dc_v = self.dma(y, "dC", self.betrow[:, :], self.d_bet.ap())
        self.dma(g, "dWq", self.b1s[:, :], self.d_b1r.ap())
        self.dma(g, "dWq", self.b2s[:, :], self.d_b2.ap())
        wg_in = bass.AP(self.d_wg, 0, [[E, 128], [128 * E, 8], [1, E]])
        self.dma(g, "dWq", bass.AP(self.wgs, 0, [[8 * E, 128], [E, 8],
                                                 [1, E]]), wg_in)
        sg = self.tag(g.memset(self.bgp[:, :], 0.0), "sG")
        self.w(g, "sG", sg)
        cst_v = self.dma(g, "dWq", self.bgp[0:1, :], self.d_bg.ap())
        self.w(y, "dSp", self.val("dSp"))

        def w1_load(it):
            e_, hc = divmod(it, 4)
            sl = it % 2
            dW = "dW0" if sl == 0 else "dW1"
            in_ = bass.AP(self.d_w1, e_ * D * HID + hc * KB,
                          [[HID, 128], [128 * HID, 8], [1, KB]])
            out = bass.AP(self.w1b, sl * 8 * KB,
                          [[2 * 8 * KB, 128], [KB, 8], [1, KB]])
            return self.dma(g, dW, out, in_)

        def w2_load(it):
            e_, hc = divmod(it, 4)
            in_ = bass.AP(self.d_w2, e_ * HID * D + hc * KB * D,
                          [[D, 128], [128 * D, 8], [1, D]])
            out = bass.AP(self.w2b, 0, [[8 * KB, 128], [KB, 8], [1, KB]])
            return self.dma(g, "dW2", out, in_)

        w1_v = {0: w1_load(0), 1: w1_load(1)}
        w2_v = {0: w2_load(0)}

        self.w(t, "dC", dc_v)
        for src, dst in ((self.gamrow, self.gamB), (self.betrow, self.betB)):
            for hh in range(2):
                pi, ps = self.pacq()
                st_g = self.tag(t.matmul(ps[:, :], self.onesrf[:, :],
                                         src[0:1, hh * 512:(hh + 1) * 512],
                                         start=True, stop=True), "sT")
                self.w(v, "sT", st_g)
                sv = self.tag(v.tensor_copy(dst[:, hh * 512:(hh + 1) * 512],
                                            ps[:, :]), "sV")
                self.prel(pi, "sV", sv)

        psx_free = 0
        self.w(g, "sG", sg)
        self.w(t, "sG", sg)
        xl_done = {}
        for tc in range(0 if "xn2" in SKIP else ntc):
            sl = tc % 2
            dL = "dL0" if sl == 0 else "dL1"
            x1c = self.x1b[:, sl * D:(sl + 1) * D]
            fsem, fval = self.x1b_free[sl]
            if fsem is not None:
                self.w(y, fsem, fval)
            vx = self.dma(y, dL, x1c,
                          bass.AP(self.d_x1sp, (seq * T + tc * 128) * D,
                                  [[D, 128], [1, D]]))
            xl_done[tc] = vx
            setn = tc % 2
            self.w(v, dL, vx)
            sv = self.ln(lambda hh, a=x1c: a[:, hh * 512:(hh + 1) * 512], setn)
            ss = self.ln_affine(setn, sv, self.x2tm[:, :], x1c)
            self.x1b_free[sl] = ("sS", ss)
            self.w(t, "sS", ss)
            self.w(t, "sS", psx_free)
            for dc in range(8):
                ins = t.transpose(self.psX[:, dc * 128:(dc + 1) * 128],
                                  self.x2tm[:, dc * 128:(dc + 1) * 128],
                                  self.ident[:, :])
            st_tr = self.tag(ins, "sT")
            self.w(s, "sT", st_tr)
            out_ap = bass.AP(self.xn2f, tc * 128,
                             [[8 * T, 128], [T, 8], [1, 128]])
            psx_free = self.tag(s.copy(out_ap, self.psX[:, :]), "sS")

            if tc == 0:
                self.w(t, "dWq", cst_v)
                self.w(t, "sG", self.val("sG"))
            pi, ps = self.pacq()
            self.w(t, "sS", psx_free)
            t.matmul(ps[:, 0:8], self.blhs[:, 0:128], self.bgp[:, :],
                     start=True, stop=False)
            for dc in range(8):
                ins = t.matmul(ps[:, 0:8],
                               self.xn2f[:, dc * T + tc * 128:
                                         dc * T + tc * 128 + 128],
                               self.wgs[:, dc * E:(dc + 1) * E],
                               start=False, stop=(dc == 7))
            st_g = self.tag(ins, "sT")
            self.w(v, "sT", st_g)
            self.edge(v, v.reduce_max(self.scs[:, 16:17], ps[:, 0:8], AX.X),
                      "sV")
            self.edge(v, v.tensor_scalar(self.eq8[:, 0:8], ps[:, 0:8],
                                         self.scs[:, 16:17], None,
                                         ALU.is_equal), "sV")
            self.edge(v, v.scalar_tensor_tensor(
                self.eq8[:, 8:16], self.eq8[:, 0:8],
                NEG, ps[:, 0:8], ALU.mult, ALU.add), "sV")
            self.edge(v, v.reduce_max(self.scs[:, 17:18], self.eq8[:, 8:16],
                                      AX.X), "sV")
            self.edge(v, v.tensor_scalar(self.eq8[:, 16:24],
                                         self.eq8[:, 8:16],
                                         self.scs[:, 17:18], None,
                                         ALU.is_equal), "sV")
            sv = self.tag(v.tensor_sub(self.scs[:, 18:19],
                                       self.scs[:, 17:18],
                                       self.scs[:, 16:17]), "sV")
            self.prel(pi, "sV", sv)
            self.w(s, "sV", sv)
            ss = self.tag(s.activation(self.scs[:, 19:20],
                                       self.scs[:, 18:19], AF.Exp), "sS")
            self.w(v, "sS", ss)
            self.edge(v, v.tensor_scalar(self.scs[:, 20:21],
                                         self.scs[:, 19:20], 1.0,
                                         None, ALU.add), "sV")
            self.edge(v, v.reciprocal(self.scs[:, 21:22],
                                      self.scs[:, 20:21]), "sV")
            self.edge(v, v.tensor_mul(self.scs[:, 22:23],
                                      self.scs[:, 19:20],
                                      self.scs[:, 21:22]), "sV")
            self.edge(v, v.tensor_scalar(self.scs[:, 23:24],
                                         self.scs[:, 22:23],
                                         -1.0, 1.0, ALU.mult, ALU.add), "sV")
            self.edge(v, v.tensor_scalar(self.eq8[:, 24:32],
                                         self.eq8[:, 16:24],
                                         self.scs[:, 22:23], None,
                                         ALU.mult), "sV")
            self.edge(v, v.scalar_tensor_tensor(
                self.comb[:, tc * E:(tc + 1) * E],
                self.eq8[:, 0:8], self.scs[:, 23:24],
                self.eq8[:, 24:32], ALU.mult, ALU.add), "sV")
            sv = self.tag(v.tensor_copy(self.combbf[:, :],
                                        self.comb[:, tc * E:(tc + 1) * E]),
                          "sV")
            self.w(t, "sV", sv)
            self.w(t, "sS", psx_free)
            st_tr = self.tag(t.transpose(self.psX[0:8, 0:128],
                                         self.combbf[:, :],
                                         self.ident[:, :]), "sT")
            self.w(s, "sT", st_tr)
            psx_free = self.tag(s.copy(self.combT[0:8, tc * 128:
                                                  tc * 128 + 128],
                                       self.psX[0:8, 0:128]), "sS")
            self.w(t, "sS", psx_free)
            for dh in range(2):
                pi, ps = self.pacq()
                st_g = self.tag(t.matmul(
                    ps[:, :], self.combT[0:8, tc * 128:tc * 128 + 128],
                    self.b2s[0:8, dh * 512:dh * 512 + 512],
                    start=True, stop=True), "sT")
                self.w(v, "sT", st_g)
                sv = self.tag(v.tensor_copy(
                    self.accf[:, tc * D + dh * 512:tc * D + dh * 512 + 512],
                    ps[:, :]), "sV")
                self.prel(pi, "sV", sv)

        if "combout" in SKIP:
            self.w(y, "sV", self.val("sV"))
            self.dma(y, "dOut0",
                     bass.AP(self.d_out, 0, [[D, 128], [1, ntc * E]]),
                     self.comb[:, :])
            self.dma(y, "dOut0",
                     bass.AP(self.d_out, 256, [[D, 128], [1, 32]]),
                     self.eq8[:, :])
            self.dma(y, "dOut0",
                     bass.AP(self.d_out, 320, [[D, 128], [1, 64]]),
                     self.scs[:, :])
            self.dma(g, "dX0",
                     bass.AP(self.d_out, 64 * D, [[D, 128], [1, 2 * D]]),
                     self.x1b[:, :])
        gelu_ss = {}
        for it in range(0 if "experts" in SKIP else niter):
            e_, hc = divmod(it, 4)
            sl = it % 2
            dW = "dW0" if sl == 0 else "dW1"
            self.w(t, dW, w1_v[it])
            self.w(t, "dW2", w2_v[it])
            last_w1 = 0
            last_w2 = 0
            for j in range(nj):
                for hs in range(8):
                    pi, ps = self.pacq()
                    for dc in range(8):
                        ins = t.matmul(
                            ps[:, :],
                            self.w1b[:, sl * 8 * KB + dc * KB + hs * 128:
                                     sl * 8 * KB + dc * KB + hs * 128 + 128],
                            self.xn2f[:, dc * T + j * 512:
                                      dc * T + j * 512 + 512],
                            start=(dc == 0), stop=(dc == 7))
                    st_g = self.tag(ins, "sT")
                    last_w1 = st_g
                    self.w(s, "sT", st_g)
                    ss = self.tag(s.activation(
                        self.gbuf[:, hs * 512:(hs + 1) * 512], ps[:, :],
                        AF.Gelu,
                        bias=self.b1s[:, e_ * 32 + hc * 8 + hs:
                                      e_ * 32 + hc * 8 + hs + 1]), "sS")
                    gelu_ss[hs] = ss
                    self.prel(pi, "sS", ss)
                self.w(t, "sS", gelu_ss[7])
                for tcl in range(4):
                    tc = 4 * j + tcl
                    for dh in range(2):
                        pi, ps = self.pacq()
                        for hs in range(8):
                            ins = t.matmul(
                                ps[:, :],
                                self.gbuf[:, hs * 512 + tcl * 128:
                                          hs * 512 + tcl * 128 + 128],
                                self.w2b[:, hs * KB + dh * 512:
                                         hs * KB + dh * 512 + 512],
                                start=(hs == 0), stop=(hs == 7))
                        st_g = self.tag(ins, "sT")
                        last_w2 = st_g
                        self.w(v, "sT", st_g)
                        sv = self.tag(v.scalar_tensor_tensor(
                            self.accf[:, tc * D + dh * 512:
                                      tc * D + dh * 512 + 512],
                            ps[:, :],
                            self.comb[:, tc * E + e_:tc * E + e_ + 1],
                            self.accf[:, tc * D + dh * 512:
                                      tc * D + dh * 512 + 512],
                            ALU.mult, ALU.add), "sV")
                        self.prel(pi, "sV", sv)
            if it + 2 < niter:
                self.w(g, "sT", last_w1)
                w1_v[it + 2] = w1_load(it + 2)
            if it + 1 < niter:
                self.w(g, "sT", last_w2)
                w2_v[it + 1] = w2_load(it + 1)

        ost_done = [0, 0]
        for tc in range(0 if "epi" in SKIP else ntc):
            sl = tc % 2
            dL = "dL0" if sl == 0 else "dL1"
            x1c = self.x1b[:, sl * D:(sl + 1) * D]
            fsem, fval = self.x1b_free[sl]
            if fsem is not None:
                self.w(y, fsem, fval)
            vx = self.dma(y, dL, x1c,
                          bass.AP(self.d_x1sp, (seq * T + tc * 128) * D,
                                  [[D, 128], [1, D]]))
            setn = tc % 2
            self.w(v, dL, vx)
            sv = self.ln(lambda hh, a=x1c: a[:, hh * 512:(hh + 1) * 512], setn)
            ss = self.ln_affine(setn, sv, self.x2tm[:, :], x1c)
            self.w(v, "sS", ss)
            v.tensor_add(self.rn[:, :], self.x2tm[:, :],
                         self.accf[:, tc * D:(tc + 1) * D])
            sv = self.ln(lambda hh: self.rn[:, hh * 512:(hh + 1) * 512], setn)
            ss = self.ln_affine(setn, sv, self.s1f[:, :], self.rn[:, :])
            self.w(v, "sS", ss)
            v.tensor_mul(self.s2f[:, :], self.s1f[:, :], self.gamB[:, :])
            v.tensor_add(self.s1f[:, :], self.s2f[:, :], self.betB[:, :])
            sv = self.ln(lambda hh: self.s1f[:, hh * 512:(hh + 1) * 512],
                         setn)
            ss = self.ln_affine(setn, sv, self.s2f[:, :], self.s1f[:, :])
            self.w(v, "sS", ss)
            dO = "dOut0" if sl == 0 else "dOut1"
            if ost_done[sl]:
                self.w(v, dO, ost_done[sl])
            sv = self.tag(v.tensor_add(self.ostg[:, sl * D:(sl + 1) * D],
                                       self.s2f[:, :], x1c), "sV")
            self.x1b_free[sl] = ("sV", sv)
            self.w(y, "sV", sv)
            ost_done[sl] = self.dma(
                y, dO,
                bass.AP(self.d_out, (seq * T + tc * 128) * D,
                        [[D, 128], [1, D]]),
                self.ostg[:, sl * D:(sl + 1) * D])


_CACHE = {}


def _get_mk(T):
    mk = _CACHE.get(T)
    if mk is None:
        mk = MK(T)
        _CACHE[T] = mk
    return mk


def kernel(**inputs):
    x = np.ascontiguousarray(np.asarray(inputs["L0_attn_x"], dtype=np.float32))
    B, T, _ = x.shape
    perm64 = _rope_perm64()
    colperm = np.arange(3 * D)
    for h in range(16):
        colperm[h * 64:(h + 1) * 64] = h * 64 + perm64
        colperm[D + h * 64:D + (h + 1) * 64] = D + h * 64 + perm64
    wqkv = np.asarray(inputs["L0_attn_W_qkv"], dtype=np.float32)
    wqkvT = np.ascontiguousarray(wqkv[colperm].T)
    woutT = np.ascontiguousarray(
        np.asarray(inputs["L0_attn_W_out"], dtype=np.float32).T)
    w1 = np.ascontiguousarray(np.asarray(inputs["L0_ffn_W1"], dtype=np.float32))
    w2 = np.ascontiguousarray(np.asarray(inputs["L0_ffn_W2"], dtype=np.float32))
    wg = np.ascontiguousarray(np.asarray(inputs["L0_ffn_Wg"], dtype=np.float32))
    bg = np.ascontiguousarray(
        np.asarray(inputs["L0_ffn_bg"], dtype=np.float32).reshape(1, E))
    b1 = np.asarray(inputs["L0_ffn_b1"], dtype=np.float32)
    b1r = np.ascontiguousarray(
        b1.reshape(E, 32, 128).transpose(2, 0, 1).reshape(128, E * 32))
    b2 = np.ascontiguousarray(np.asarray(inputs["L0_ffn_b2"], dtype=np.float32))
    gam = np.ascontiguousarray(
        np.asarray(inputs["L0_ffn_gamma"], dtype=np.float32).reshape(1, D))
    bet = np.ascontiguousarray(
        np.asarray(inputs["L0_ffn_beta"], dtype=np.float32).reshape(1, D))
    ropeA, ropeB = _rope_tables(T)
    tri = np.asarray(
        (np.arange(128)[:, None] <= np.arange(128)[None, :]),
        dtype=ml_dtypes.bfloat16)

    mk = _get_mk(T)
    shared = dict(wqkvT=wqkvT, woutT=woutT, w1=w1, w2=w2, wg=wg, bg=bg,
                  b1r=b1r, b2=b2, gam=gam, bet=bet, ropeA=ropeA, ropeB=ropeB,
                  tri=tri)
    ncore = B // NSEQ
    in_maps = []
    for c in range(ncore):
        m = dict(shared)
        m["x"] = np.ascontiguousarray(x[c * NSEQ:(c + 1) * NSEQ])
        in_maps.append(m)
    res = run_bass_kernel_spmd(mk.nc, in_maps, core_ids=list(range(ncore)))
    return np.concatenate([r["out"] for r in res.results], axis=0)
